# version 01 (final)
# baseline (speedup 1.0000x reference)
import math
import numpy as np
import concourse.bass as bass
import concourse.mybir as mybir
from concourse.bass_utils import run_bass_kernel_spmd

F32 = mybir.dt.float32
BF16 = mybir.dt.bfloat16
AF = mybir.ActivationFunctionType
ALU = mybir.AluOpType
AX = mybir.AxisListType.X

NCORES = 8
B, T, D = 2, 8192, 1024
NJ = 4
TL = T // NJ
TT = 512
NTT = TL // TT
NC = D // 128
H, HDIM = 16, 64
FF = 4096
NE, EI = 8, 2816
CHUNK = 128
NCH = TL // CHUNK
EPS = 1e-5

TENSORS = {}


def reg(name, shape, dtype):
    TENSORS[name] = (list(shape), dtype)
    return name


reg("x_fm", (D, TL), F32)
reg("w1T", (D, 3 * D), F32)
reg("w2T", (D, D), F32)
reg("fw1T", (D, FF), F32)
reg("fw2T", (D, FF), F32)
reg("fh1T", (FF, FF), F32)
reg("fh2T", (FF, FF), F32)
reg("fw3T", (FF, D), F32)
reg("fs1", (128, FF // 128), F32)
reg("fs2", (128, FF // 128), F32)
reg("fsh1", (128, FF // 128), F32)
reg("fsh2", (128, FF // 128), F32)
reg("fs3", (128, NC), F32)
reg("ln1w", (128, NC), F32)
reg("ln1b", (128, NC), F32)
reg("convw", (128, NC, 4), F32)
reg("convb", (128, NC), F32)
reg("qkT", (D, D), F32)
reg("vT", (D, D), F32)
reg("gT", (D, D), F32)
reg("gb", (128, NC), F32)
reg("oT", (D, D), F32)
reg("onw", (128, NC), F32)
reg("onb", (128, NC), F32)
reg("gammaA", (16, 1), F32)
reg("ln2w", (128, NC), F32)
reg("ln2b", (128, NC), F32)
reg("gateT", (D, FF), F32)
reg("upT", (D, FF), F32)
reg("downT", (FF, D), F32)
reg("gamma1", (128, NC), F32)
reg("gamma2", (128, NC), F32)
reg("routT", (D, NE), F32)
for e in range(NE):
    reg(f"wgT{e}", (D, EI), F32)
    reg(f"wuT{e}", (D, EI), F32)
    reg(f"wdT{e}", (EI, D), F32)

reg("cos_fm", (128, NC, TL), F32)
reg("t16", (16, TL), F32)
reg("triu", (128, 128), BF16)
reg("rankmask", (128, 4), F32)
reg("halosel", (128, 4), F32)
reg("haloflag", (128, 1), F32)
reg("ident", (128, 128), F32)
reg("ind_bc", (16, NC * 128), F32)
reg("ind_sq", (128, NC * 16), BF16)

reg("xn_bf", (D, TL), BF16)
reg("xn2_bf", (D, TL), BF16)
reg("qf", (D, TL), BF16)
reg("kf", (D, TL), BF16)
reg("ktm", (TL, D), BF16)
reg("vtm", (TL, D), BF16)
reg("y_bf", (D, TL), BF16)
reg("s0", (D, TL), F32)
reg("h_bf", (FF, TL), BF16)
reg("m1_bf", (FF, TL), BF16)
reg("m2_bf", (FF, TL), BF16)
reg("x1", (D, TL), F32)
reg("xn1f", (D, TL), F32)
reg("xn1_bf", (D, TL), BF16)
reg("xc_bf", (D, TL), BF16)
reg("xch_bf", (128, 2 * NC), BF16)
reg("sdt", (D, TL), F32)
reg("qL1", (D, TL), F32)
reg("carry", (128, NC), F32)
reg("dbg1", (128, NC), F32)
reg("dbg2", (128, 4 * NC), F32)
reg("dbg3", (128, 4 * NE), F32)
reg("dbg4", (128, 4 * NE), F32)
reg("dbg5", (128, 4 * 4), F32)
reg("at1_bf", (D, TL), BF16)
reg("xmid", (D, TL), F32)
reg("h2_bf", (D, TL), BF16)
reg("m_bf", (FF, TL), BF16)
reg("sl1", (D, TL), F32)
reg("comb", (NE, TL), F32)
reg("moe", (D, TL), F32)
reg("out_fm", (D, TL), F32)

CC_TENSORS = {
    "cc_s_in": ([64, 16 * 65], F32),
    "cc_s_out": ([4 * 64, 16 * 65], F32),
    "cc_sd_in": ([128, NC], F32),
    "cc_sd_out": ([4 * 128, NC], F32),
    "cc_h_in": ([D, 4], F32),
    "cc_h_out": ([4 * D, 4], F32),
}

GROUPS = [[0, 1, 2, 3], [4, 5, 6, 7]]


class Ctx:
    ENGS = ("sync", "tensor", "vector", "scalar", "gpsimd")

    def __init__(self, nc):
        self.nc = nc
        self.cmds = {e: [] for e in self.ENGS}
        self.sems = {}
        self.cnt = {}
        for s in ("ld", "mm", "ve", "ac", "st", "cc"):
            self.sems[s] = nc.semaphore("sem_" + s).__enter__()
            self.cnt[s] = 0
        self.aps = {}
        self._sbufs = {}
        self._psums = {}
        self._uid = 0
        self._consts = {}

    def constcol(self, value, parts=128, dtype=F32):
        key = (float(value), parts, dtype)
        if key not in self._consts:
            t = self.sbuf("const", (parts, 1), dtype)
            self.run("vector", lambda e, t=t, v=value: e.memset(t[:], v), inc="ve")
            self._consts[key] = t
        return self._consts[key]

    def uid(self):
        self._uid += 1
        return self._uid

    def sbuf(self, name, shape, dtype):
        cm = self.nc.sbuf_tensor(f"sb_{name}_{self.uid()}", list(shape), dtype)
        h = cm.__enter__()
        self._guards = getattr(self, "_guards", {})
        self._guards[h.name] = cm
        return h

    def sbuf_free(self, h):
        self._guards.pop(h.name).__exit__(None, None, None)

    def psum(self, name, shape, dtype=F32):
        h = self.nc.psum_tensor(f"ps_{name}_{self.uid()}", list(shape), dtype).__enter__()
        return h

    def rec(self, eng, fn):
        self.cmds[eng].append(fn)

    def wait(self, eng, s, val=None):
        v = self.cnt[s] if val is None else val
        sem = self.sems[s]
        self.rec(eng, lambda e, sem=sem, v=v: e.wait_ge(sem, v))

    def run(self, eng, barefn, inc=None, amt=1):
        if inc is None:
            self.rec(eng, barefn)
            return self.cnt
        sem = self.sems[inc]
        self.cnt[inc] += amt
        self.rec(eng, lambda e, fn=barefn, sem=sem, amt=amt: fn(e).then_inc(sem, amt))
        return self.cnt[inc]

    def dma(self, eng, out, in_, **kw):
        return self.run(eng, lambda e, o=out, i=in_, kw=kw: e.dma_start(out=o, in_=i, **kw),
                        inc="ld" if eng != "__store" else "st", amt=16)

    def store(self, out, in_, eng="sync"):
        return self.run(eng, lambda e, o=out, i=in_: e.dma_start(out=o, in_=i),
                        inc="st", amt=16)

    def barrier(self):
        for eng in self.ENGS:
            for s in ("ld", "mm", "ve", "ac", "st", "cc"):
                if self.cnt[s] > 0:
                    self.wait(eng, s)


STAGES = []


def stage(name, reads, writes, cc=()):
    def deco(fn):
        STAGES.append((name, list(reads), list(writes), fn, list(cc)))
        return fn
    return deco


def fm_view(ap, nchunks):
    return ap.rearrange("(c p) t -> p c t", p=128)


def emit_ones(ctx, dtype=F32, shape=(128, 1), const=1.0):
    t = ctx.sbuf("ones", shape, dtype)
    ctx.run("vector", lambda e, t=t, c=const: e.memset(t[:], c), inc="ve")
    return t


def load_weight_cache(ctx, name, I, O, o0=None, o1=None, dtype=BF16):
    if o0 is None:
        o0, o1 = 0, O
    kc = I // 128
    wc = ctx.sbuf(name, (128, kc, o1 - o0), dtype)
    src = ctx.aps[name.split("$")[0]].rearrange("(c p) o -> p c o", p=128)[:, :, o0:o1]
    ctx.dma("gpsimd", wc[:], src)
    return wc


def ln_rows(ctx, ps_sum, ps_sq, rows, n_feat):
    W = rows.shape[2] if len(rows.shape) == 3 else rows.shape[1]
    raise NotImplementedError


def emit_layernorm(ctx, src, W, dst_ops, ones_f, onesr_f, psums, rows, sq,
                   eps=EPS):
    ps_sum, ps_sq, ps_bcr, ps_bcn = psums
    ctx.wait("scalar", "ve")
    ctx.wait("scalar", "ld")
    for c in range(NC):
        ctx.run("scalar", lambda e, c=c: e.activation(sq[:, c, :], src(c), AF.Square),
                inc="ac" if c == NC - 1 else None)
    ac_sq = ctx.cnt["ac"]
    ctx.wait("tensor", "ld")
    ctx.wait("tensor", "ve")
    ctx.wait("tensor", "ac", ac_sq)
    for c in range(NC):
        ctx.run("tensor", lambda e, c=c: e.matmul(
            ps_sum[0:1, :], ones_f[:], src(c), start=(c == 0), stop=(c == NC - 1)))
        ctx.run("tensor", lambda e, c=c: e.matmul(
            ps_sq[0:1, :], ones_f[:], sq[:, c, :], start=(c == 0), stop=(c == NC - 1)),
            inc="mm" if c == NC - 1 else None)
    mm_stat = ctx.cnt["mm"]
    ctx.wait("vector", "mm", mm_stat)
    ctx.run("vector", lambda e: e.tensor_scalar_mul(rows[:, 2, :], ps_sum[0:1, :], 1.0 / D))
    ctx.run("vector", lambda e: e.tensor_scalar_mul(rows[:, 3, :], ps_sq[0:1, :], 1.0 / D))
    ctx.run("vector", lambda e: e.tensor_tensor(rows[:, 7, :], rows[:, 2, :], rows[:, 2, :], ALU.mult))
    ctx.run("vector", lambda e: e.tensor_sub(rows[:, 3, :], rows[:, 3, :], rows[:, 7, :]), inc="ve")
    eps_c = ctx.constcol(eps, parts=1)
    ctx.wait("scalar", "ve")
    ctx.run("scalar", lambda e, eps_c=eps_c: e.activation(rows[:, 4, :], rows[:, 3, :], AF.Sqrt, bias=eps_c[:]), inc="ac")
    ac_std = ctx.cnt["ac"]
    ctx.wait("vector", "ac", ac_std)
    ctx.run("vector", lambda e: e.reciprocal(rows[:, 5, :], rows[:, 4, :]))
    ctx.run("vector", lambda e: e.tensor_tensor(rows[:, 6, :], rows[:, 2, :], rows[:, 5, :], ALU.mult))
    ctx.run("vector", lambda e: e.tensor_scalar_mul(rows[:, 6, :], rows[:, 6, :], -1.0), inc="ve")
    ve_rows = ctx.cnt["ve"]
    ctx.wait("tensor", "ve", ve_rows)
    ctx.run("tensor", lambda e: e.matmul(ps_bcr[:, :], onesr_f[:], rows[:, 5, :], start=True, stop=True))
    ctx.run("tensor", lambda e: e.matmul(ps_bcn[:, :], onesr_f[:], rows[:, 6, :], start=True, stop=True), inc="mm")
    mm_bc = ctx.cnt["mm"]
    ctx.wait("vector", "mm", mm_bc)
    for c in range(NC):
        dst_ops(c, ps_bcr, ps_bcn)
    return ctx.cnt["ve"]


@stage("ln0", ["x_fm"], ["xn_bf", "xn2_bf"])
def emit_ln0(ctx):
    x_sb = ctx.sbuf("x", (128, NC, TL), F32)
    ctx.dma("sync", x_sb[:], fm_view(ctx.aps["x_fm"], NC))

    ones_f = emit_ones(ctx, F32, (128, 1))
    onesr_f = emit_ones(ctx, F32, (1, 128))

    sq = ctx.sbuf("sq", (128, NC, TT), F32)
    xn = ctx.sbuf("xn", (128, NC, TT), F32)
    rows = ctx.sbuf("rows", (1, 8, TT), F32)
    ps_sum = ctx.psum("sum", (1, TT), F32)
    ps_sq = ctx.psum("sq", (1, TT), F32)
    ps_bcr = ctx.psum("bcr", (128, TT), F32)
    ps_bcn = ctx.psum("bcn", (128, TT), F32)
    outA = ctx.sbuf("outA", (128, NC, TT), BF16)
    outB = ctx.sbuf("outB", (128, NC, TT), BF16)

    xnbf_v = fm_view(ctx.aps["xn_bf"], NC)
    xn2bf_v = fm_view(ctx.aps["xn2_bf"], NC)
    psums = (ps_sum, ps_sq, ps_bcr, ps_bcn)
    for tt in range(NTT):
        sl = slice(tt * TT, (tt + 1) * TT)

        def apply1(c, bcr, bcn, sl=sl):
            ctx.run("vector", lambda e, c=c: e.tensor_tensor(xn[:, c, :], x_sb[:, c, sl], bcr[:, :], ALU.mult))
            ctx.run("vector", lambda e, c=c: e.tensor_tensor(xn[:, c, :], xn[:, c, :], bcn[:, :], ALU.add))
            ctx.run("vector", lambda e, c=c: e.tensor_copy(outA[:, c, :], xn[:, c, :]), inc="ve")

        ve1 = emit_layernorm(ctx, lambda c, sl=sl: x_sb[:, c, sl], TT, apply1,
                             ones_f, onesr_f, psums, rows, sq)
        ctx.wait("sync", "ve", ve1)
        ctx.store(xnbf_v[:, :, sl], outA[:])

        def apply2(c, bcr, bcn):
            ctx.run("vector", lambda e, c=c: e.tensor_tensor(sq[:, c, :], xn[:, c, :], bcr[:, :], ALU.mult))
            ctx.run("vector", lambda e, c=c: e.tensor_tensor(outB[:, c, :], sq[:, c, :], bcn[:, :], ALU.add), inc="ve")

        ve2 = emit_layernorm(ctx, lambda c: xn[:, c, :], TT, apply2,
                             ones_f, onesr_f, psums, rows, sq)
        ctx.wait("sync", "ve", ve2)
        st2 = ctx.store(xn2bf_v[:, :, sl], outB[:])
        ctx.wait("vector", "st", st2)
        ctx.wait("scalar", "st", st2)
        ctx.wait("tensor", "st", st2)


@stage("qkv", ["xn2_bf", "w1T"], ["qf", "kf", "ktm", "vtm"])
def emit_qkv(ctx):
    xn2 = ctx.sbuf("xn2", (128, NC, TL), BF16)
    ctx.dma("sync", xn2[:], fm_view(ctx.aps["xn2_bf"], NC))
    wc = load_weight_cache(ctx, "w1T", D, 3 * D)
    ld0 = ctx.cnt["ld"]

    ps = [ctx.psum(f"mm{i}", (128, TT), F32) for i in range(2)]
    stage_o = ctx.sbuf("stg", (128, 2, TT), BF16)
    tmp_r = ctx.sbuf("tmpr", (128, 2, TT), F32)
    tmp_e = ctx.sbuf("tmpe", (128, 2, TT), F32)

    qf_v = fm_view(ctx.aps["qf"], NC)
    kf_v = fm_view(ctx.aps["kf"], NC)

    ctx.wait("tensor", "ld", ld0)

    def elu1_evac(psum, buf, dst_view, with_elu=True):
        b = buf
        if with_elu:
            ctx.run("scalar", lambda e: e.activation(tmp_r[:, b, :], psum[:], AF.Relu, scale=-1.0), inc="ac")
            ctx.wait("scalar", "ac")
            ctx.run("scalar", lambda e: e.activation(tmp_e[:, b, :], tmp_r[:, b, :], AF.Exp, scale=-1.0))
            ctx.run("scalar", lambda e: e.activation(tmp_r[:, b, :], psum[:], AF.Relu), inc="ac")
            ac1 = ctx.cnt["ac"]
            ctx.wait("vector", "ac", ac1)
            ctx.run("vector", lambda e: e.tensor_add(stage_o[:, b, :], tmp_e[:, b, :], tmp_r[:, b, :]), inc="ve")
        else:
            ctx.run("vector", lambda e: e.tensor_copy(stage_o[:, b, :], psum[:]), inc="ve")
        ve1 = ctx.cnt["ve"]
        ctx.wait("sync", "ve", ve1)
        st = ctx.store(dst_view, stage_o[:, b, :])
        return st

    last_st = 0
    for part, dstv, elu in (("q", qf_v, True), ("k", kf_v, True)):
        obase = 0 if part == "q" else NC
        for oc in range(NC):
            for tt in range(NTT):
                pp = ps[(oc * NTT + tt) % 2]
                ctx.wait("tensor", "ve")
                ctx.wait("tensor", "st", last_st)
                for kcc in range(NC):
                    ctx.run("tensor", lambda e, kcc=kcc, oc=oc, tt=tt, pp=pp, obase=obase: e.matmul(
                        pp[:], wc[:, kcc, (obase + oc) * 128:(obase + oc + 1) * 128],
                        xn2[:, kcc, tt * TT:(tt + 1) * TT],
                        start=(kcc == 0), stop=(kcc == NC - 1)),
                        inc="mm" if kcc == NC - 1 else None)
                ctx.wait("scalar", "mm")
                ctx.wait("vector", "mm")
                last_st = elu1_evac(pp, (oc * NTT + tt) % 2, dstv[:, oc, tt * TT:(tt + 1) * TT], elu)

    ps_k = [ctx.psum(f"ktm{i}", (128, TT), F32) for i in range(2)]
    ps_v = [ctx.psum(f"vtm{i}", (128, TT), F32) for i in range(2)]
    ktm_stage = ctx.sbuf("kst", (128, D), BF16)
    vtm_stage = ctx.sbuf("vst", (128, D), BF16)
    ktm_ap = ctx.aps["ktm"]
    vtm_ap = ctx.aps["vtm"]
    for tc in range(TL // 128):
        ctx.wait("tensor", "st", last_st)
        for half in range(2):
            for kcc in range(NC):
                ctx.run("tensor", lambda e, kcc=kcc, tc=tc, half=half: e.matmul(
                    ps_k[half][:], xn2[:, kcc, tc * 128:(tc + 1) * 128],
                    wc[:, kcc, D + half * TT: D + (half + 1) * TT],
                    start=(kcc == 0), stop=(kcc == NC - 1)))
                ctx.run("tensor", lambda e, kcc=kcc, tc=tc, half=half: e.matmul(
                    ps_v[half][:], xn2[:, kcc, tc * 128:(tc + 1) * 128],
                    wc[:, kcc, 2 * D + half * TT: 2 * D + (half + 1) * TT],
                    start=(kcc == 0), stop=(kcc == NC - 1)),
                    inc="mm" if kcc == NC - 1 else None)
        mm_t = ctx.cnt["mm"]
        ctx.wait("scalar", "mm", mm_t)
        ctx.wait("vector", "mm", mm_t)
        for half in range(2):
            sl = slice(half * TT, (half + 1) * TT)
            ctx.run("scalar", lambda e, half=half, sl=sl: e.activation(tmp_r[:, half, :], ps_k[half][:], AF.Relu, scale=-1.0))
            ctx.run("scalar", lambda e, half=half, sl=sl: e.activation(tmp_e[:, half, :], tmp_r[:, half, :], AF.Exp, scale=-1.0))
            ctx.run("scalar", lambda e, half=half, sl=sl: e.activation(tmp_r[:, half, :], ps_k[half][:], AF.Relu), inc="ac")
            ctx.wait("vector", "ac")
            ctx.run("vector", lambda e, half=half, sl=sl: e.tensor_add(ktm_stage[:, sl], tmp_e[:, half, :], tmp_r[:, half, :]))
            ctx.run("vector", lambda e, half=half, sl=sl: e.tensor_copy(vtm_stage[:, sl], ps_v[half][:]), inc="ve")
        ve_t = ctx.cnt["ve"]
        ctx.wait("sync", "ve", ve_t)
        ctx.store(ktm_ap[tc * 128:(tc + 1) * 128, :], ktm_stage[:])
        last_st = ctx.store(vtm_ap[tc * 128:(tc + 1) * 128, :], vtm_stage[:])
        ctx.wait("scalar", "st", last_st)
        ctx.wait("vector", "st", last_st)


@stage("l0scan", ["qf", "kf", "ktm", "vtm", "triu", "rankmask"], ["y_bf"],
       cc=("cc_s_in", "cc_s_out"))
def emit_l0scan(ctx):
    nc = ctx.nc
    ccsem = nc.semaphore("sem_ccl0").__enter__()

    qf_sb = ctx.sbuf("qf", (128, NC, TL), BF16)
    kf_sb = ctx.sbuf("kf", (128, NC, TL), BF16)
    ktm_sb = ctx.sbuf("ktm", (128, NCH, D), BF16)
    vtm_sb = ctx.sbuf("vtm", (128, NCH, D), BF16)
    triu_sb = ctx.sbuf("triu", (128, 128), BF16)
    rkm_sb = ctx.sbuf("rkm", (128, 4), F32)
    ctx.dma("sync", qf_sb[:], fm_view(ctx.aps["qf"], NC))
    ctx.dma("sync", kf_sb[:], fm_view(ctx.aps["kf"], NC))
    ctx.dma("sync", ktm_sb[:], ctx.aps["ktm"].rearrange("(tc p) d -> p tc d", p=128))
    ctx.dma("sync", vtm_sb[:], ctx.aps["vtm"].rearrange("(tc p) d -> p tc d", p=128))
    ctx.dma("sync", triu_sb[:], ctx.aps["triu"])
    ctx.dma("sync", rkm_sb[:], ctx.aps["rankmask"])
    ld0 = ctx.cnt["ld"]

    ones_cb = emit_ones(ctx, BF16, (128, 1))
    ones_r64 = emit_ones(ctx, F32, (1, 64))

    SW = H * 65
    sloc = ctx.sbuf("sloc", (64, SW), F32)
    gath = ctx.sbuf("gath", (64, 4, SW), F32)
    prefA = ctx.sbuf("prefA", (64, SW), F32)
    prefB = ctx.sbuf("prefB", (64, SW), F32)
    srun = [ctx.sbuf(f"srun{i}", (64, SW), F32) for i in range(2)]
    sz2 = [ctx.sbuf(f"sz2_{p}", (128, 65), BF16) for p in range(2)]
    kf2 = [[ctx.sbuf(f"kf2_{p}{i}", (128, 128), BF16) for i in range(2)] for p in range(2)]
    atm = [ctx.sbuf(f"atm{i}", (128, TT), BF16) for i in range(2)]
    rden = ctx.sbuf("rden", (1, 2, TT), F32)
    bc_sb = ctx.sbuf("bcs", (64, TT), F32)
    ybuf = ctx.sbuf("ybuf", (128, NC, TL), BF16)

    ps_at = [ctx.psum(f"at{i}", (128, TT), F32) for i in range(2)]
    ps_num = ctx.psum("num", (64, TT), F32)
    ps_den = ctx.psum("den", (1, TT), F32)
    ps_kv = ctx.psum("kv", (64, 65), F32)
    ps_bcd = ctx.psum("bcd", (64, TT), F32)

    for p in range(2):
        ctx.run("vector", lambda e, p=p: e.memset(sz2[p][:], 0.0))
        for i in range(2):
            ctx.run("vector", lambda e, p=p, i=i: e.memset(kf2[p][i][:], 0.0))
    ctx.run("vector", lambda e: e.memset(prefA[:], 0.0), inc="ve")

    ctx.wait("tensor", "ld", ld0)
    ctx.wait("tensor", "ve")
    ctx.wait("scalar", "ld", ld0)

    for h in range(H):
        hc, hp = h // 2, h % 2
        hsl = slice(h * 64, (h + 1) * 64)
        ctx.wait("tensor", "ve")
        for sc in range(NCH):
            ctx.run("tensor", lambda e, sc=sc, hsl=hsl: e.matmul(
                ps_kv[:, 0:64], ktm_sb[:, sc, hsl], vtm_sb[:, sc, hsl],
                start=(sc == 0), stop=False))
            ctx.run("tensor", lambda e, sc=sc, hsl=hsl: e.matmul(
                ps_kv[:, 64:65], ktm_sb[:, sc, hsl], ones_cb[:],
                start=False, stop=(sc == NCH - 1)),
                inc="mm" if sc == NCH - 1 else None)
        ctx.wait("vector", "mm")
        ctx.run("vector", lambda e, h=h: e.tensor_copy(sloc[:, h * 65:(h + 1) * 65], ps_kv[:]), inc="ve")

    ctx.wait("sync", "ve")
    st_sloc = ctx.store(ctx.aps["cc_s_in"], sloc[:])
    ctx.wait("gpsimd", "st", st_sloc)
    ctx.rec("gpsimd", lambda e, s=ccsem: e.collective_compute(
        "AllGather", ALU.bypass, replica_groups=GROUPS,
        ins=[ctx.aps["cc_s_in"].opt()], outs=[ctx.aps["cc_s_out"].opt()]).then_inc(s))
    ctx.rec("sync", lambda e, s=ccsem: e.wait_ge(s, 1))
    ctx.dma("sync", gath[:], ctx.aps["cc_s_out"].rearrange("(r p) f -> p r f", p=64))
    ld_g = ctx.cnt["ld"]
    ctx.wait("vector", "ld", ld_g)
    pa, pb = prefA, prefB
    for r in range(4):
        ctx.run("vector", lambda e, r=r, pa=pa, pb=pb: e.scalar_tensor_tensor(
            pb[:], gath[:, r, :], rkm_sb[0:64, r:r + 1], pa[:], ALU.mult, ALU.add),
            inc="ve" if r == 3 else None)
        pa, pb = pb, pa
    ctx.run("vector", lambda e, pa=pa: e.tensor_copy(srun[0][:], pa[:]), inc="ve")

    yv = fm_view(ctx.aps["y_bf"], NC)
    for h in range(H):
        hc, hp = h // 2, h % 2
        hsl = slice(h * 64, (h + 1) * 64)
        ssl = slice(h * 65, (h + 1) * 65)
        psl = slice(hp * 64, hp * 64 + 64)
        for tt in range(NTT):
            cur, nxt = srun[tt % 2], srun[(tt + 1) % 2]
            tsl = slice(tt * TT, (tt + 1) * TT)
            ctx.wait("scalar", "mm")
            ctx.wait("scalar", "ve")
            ctx.run("scalar", lambda e, ssl=ssl, psl=psl, hp=hp, cur=cur: e.copy(
                sz2[hp][psl, :], cur[:, ssl]), inc="ac")
            ctx.wait("tensor", "ac")
            ctx.wait("tensor", "ve")
            ctx.run("tensor", lambda e, hp=hp, hc=hc, tsl=tsl: e.matmul(
                ps_num[:, :], sz2[hp][:, 0:64], qf_sb[:, hc, tsl], start=True, stop=False))
            ctx.run("tensor", lambda e, hp=hp, hc=hc, tsl=tsl: e.matmul(
                ps_den[:, :], sz2[hp][:, 64:65], qf_sb[:, hc, tsl], start=True, stop=False))
            for j in range(4):
                sc = tt * 4 + j
                nj = TT - j * 128
                ctx.wait("scalar", "mm")
                ctx.run("scalar", lambda e, hp=hp, j=j, hc=hc, sc=sc, psl=psl: e.copy(
                    kf2[hp][j % 2][psl, :], kf_sb[psl, hc, sc * 128:(sc + 1) * 128]), inc="ac")
                ctx.wait("tensor", "ac")
                ctx.wait("tensor", "ve")
                ctx.run("tensor", lambda e, hp=hp, j=j, hc=hc, tt=tt, nj=nj: e.matmul(
                    ps_at[j % 2][:, 0:nj], kf2[hp][j % 2][:],
                    qf_sb[:, hc, tt * TT + j * 128:(tt + 1) * TT],
                    start=True, stop=True), inc="mm")
                ctx.wait("vector", "mm")
                if nj > 128:
                    ctx.run("vector", lambda e, j=j, nj=nj: e.tensor_copy(
                        atm[j % 2][:, 128:nj], ps_at[j % 2][:, 128:nj]))
                ctx.run("vector", lambda e, j=j: e.tensor_tensor(
                    atm[j % 2][:, 0:128], ps_at[j % 2][:, 0:128], triu_sb[:], ALU.mult),
                    inc="ve")
                ctx.wait("tensor", "ve")
                ctx.run("tensor", lambda e, sc=sc, hsl=hsl, j=j, nj=nj: e.matmul(
                    ps_num[:, j * 128:TT], vtm_sb[:, sc, hsl], atm[j % 2][:, 0:nj],
                    start=False, stop=False))
                ctx.run("tensor", lambda e, j=j, nj=nj: e.matmul(
                    ps_den[0:1, j * 128:TT], ones_cb[:], atm[j % 2][:, 0:nj],
                    start=False, stop=(j == 3)))
                ctx.run("tensor", lambda e, sc=sc, hsl=hsl, j=j: e.matmul(
                    ps_kv[:, 0:64], ktm_sb[:, sc, hsl], vtm_sb[:, sc, hsl],
                    start=(j == 0), stop=False))
                ctx.run("tensor", lambda e, sc=sc, hsl=hsl, j=j: e.matmul(
                    ps_kv[:, 64:65], ktm_sb[:, sc, hsl], ones_cb[:],
                    start=False, stop=(j == 3)),
                    inc="mm" if j == 3 else None)
            mm_tile = ctx.cnt["mm"]
            ctx.wait("vector", "mm", mm_tile)
            ctx.run("vector", lambda e, tt=tt: e.tensor_copy(rden[:, 0, :], ps_den[0:1, :]))
            ctx.run("vector", lambda e, tt=tt: e.reciprocal(rden[:, 1, :], rden[:, 0, :]), inc="ve")
            ctx.wait("tensor", "ve")
            ctx.run("tensor", lambda e: e.matmul(
                ps_bcd[:], ones_r64[:], rden[:, 1, :], start=True, stop=True), inc="mm")
            ctx.wait("vector", "mm")
            ctx.run("vector", lambda e: e.tensor_copy(bc_sb[:], ps_bcd[:]))
            ctx.run("vector", lambda e, hp=hp, hc=hc, tsl=tsl: e.tensor_tensor(
                ybuf[hp * 64:hp * 64 + 64, hc, tsl], ps_num[:], bc_sb[:], ALU.mult))
            ctx.run("vector", lambda e, cur=cur, nxt=nxt, ssl=ssl: e.tensor_tensor(
                nxt[:, ssl], cur[:, ssl], ps_kv[:], ALU.add), inc="ve")

    ctx.wait("sync", "ve")
    ctx.store(yv[:, :, :], ybuf[:])


def emit_proj_stage(ctx, inname, K, O, wnames, opanel, evac, pre_tt=None):
    kc = K // 128
    in_full = kc <= 8
    if in_full:
        in_sb = ctx.sbuf("pin", (128, kc, TL), BF16)
        ctx.dma("sync", in_sb[:], fm_view(ctx.aps[inname], kc))
    else:
        in_sb = ctx.sbuf("pin", (128, kc, TT), BF16)
    wcs = [ctx.sbuf(f"wc{i}", (128, kc, opanel), BF16) for i in range(len(wnames))]
    ps = [ctx.psum(f"pp{i}", (128, TT), F32) for i in range(2 * len(wnames))]
    nw = len(wnames)
    for p0 in range(0, O, opanel):
        ctx.wait("gpsimd", "mm")
        for i, wn in enumerate(wnames):
            src = ctx.aps[wn].rearrange("(c p) o -> p c o", p=128)[:, :, p0:p0 + opanel]
            ctx.dma("gpsimd", wcs[i][:], src)
        ld_w = ctx.cnt["ld"]
        for tt in range(NTT):
            if not in_full:
                ctx.wait("sync", "mm")
                ctx.dma("sync", in_sb[:],
                        fm_view(ctx.aps[inname], kc)[:, :, tt * TT:(tt + 1) * TT])
            if pre_tt is not None and p0 == 0:
                pre_tt(tt)
            ld_in = ctx.cnt["ld"]
            for ocl in range(opanel // 128):
                oc = p0 // 128 + ocl
                par = (ocl * NTT + tt) % 2
                ctx.wait("tensor", "ld", max(ld_w, ld_in))
                ctx.wait("tensor", "ve")
                ctx.wait("tensor", "ac")
                for i in range(nw):
                    pp = ps[2 * i + par]
                    for kcc in range(kc):
                        rhs_sl = (slice(tt * TT, (tt + 1) * TT) if in_full
                                  else slice(0, TT))
                        ctx.run("tensor", lambda e, i=i, kcc=kcc, pp=pp, ocl=ocl, rhs_sl=rhs_sl: e.matmul(
                            pp[:], wcs[i][:, kcc, ocl * 128:(ocl + 1) * 128],
                            in_sb[:, kcc, rhs_sl],
                            start=(kcc == 0), stop=(kcc == kc - 1)),
                            inc="mm" if (kcc == kc - 1 and i == nw - 1) else None)
                evac(oc, tt, [ps[2 * i + par] for i in range(nw)], par)


def load_scale(ctx, name):
    shp, dt = TENSORS[name]
    t = ctx.sbuf(name, shp, dt)
    ctx.dma("sync", t[:], ctx.aps[name])
    return t


@stage("w2", ["y_bf", "w2T"], ["s0"])
def emit_w2(ctx):
    stg = ctx.sbuf("stg", (128, 2, TT), F32)
    outv = fm_view(ctx.aps["s0"], NC)

    def ev(oc, tt, pps, par):
        ctx.wait("vector", "mm")
        ctx.wait("vector", "st")
        ctx.run("vector", lambda e, par=par: e.tensor_copy(stg[:, par, :], pps[0][:]), inc="ve")
        ctx.wait("sync", "ve")
        ctx.store(outv[:, oc, tt * TT:(tt + 1) * TT], stg[:, par, :])

    emit_proj_stage(ctx, "y_bf", D, D, ["w2T"], D, ev)


@stage("ffa", ["xn_bf", "fw1T", "fw2T", "fs1", "fs2"], ["h_bf"])
def emit_ffa(ctx):
    s1 = load_scale(ctx, "fs1")
    s2 = load_scale(ctx, "fs2")
    stgA = ctx.sbuf("stgA", (128, 2, TT), F32)
    stgB = ctx.sbuf("stgB", (128, 2, TT), F32)
    stgO = ctx.sbuf("stgO", (128, 2, TT), BF16)
    outv = fm_view(ctx.aps["h_bf"], FF // 128)

    def ev(oc, tt, pps, par):
        ctx.wait("scalar", "mm")
        ctx.wait("scalar", "st")
        ctx.run("scalar", lambda e, par=par, oc=oc: e.activation(
            stgA[:, par, :], pps[0][:], AF.Silu, scale=s1[:, oc:oc + 1]), inc="ac")
        ctx.wait("vector", "mm")
        ctx.wait("vector", "st")
        ctx.run("vector", lambda e, par=par, oc=oc: e.tensor_scalar(
            stgB[:, par, :], pps[1][:], s2[:, oc:oc + 1], None, ALU.mult))
        ctx.wait("vector", "ac")
        ctx.run("vector", lambda e, par=par: e.tensor_tensor(
            stgO[:, par, :], stgA[:, par, :], stgB[:, par, :], ALU.mult), inc="ve")
        ctx.wait("sync", "ve")
        ctx.store(outv[:, oc, tt * TT:(tt + 1) * TT], stgO[:, par, :])

    emit_proj_stage(ctx, "xn_bf", D, FF, ["fw1T", "fw2T"], 2048, ev)


def _gelu_stage(ctx, inname, wname, sname, outname):
    sc = load_scale(ctx, sname)
    stgO = ctx.sbuf("stgO", (128, 2, TT), BF16)
    outv = fm_view(ctx.aps[outname], FF // 128)

    def ev(oc, tt, pps, par):
        ctx.wait("scalar", "mm")
        ctx.wait("scalar", "st")
        ctx.run("scalar", lambda e, par=par, oc=oc: e.activation(
            stgO[:, par, :], pps[0][:], AF.Gelu, scale=sc[:, oc:oc + 1]), inc="ac")
        ctx.wait("sync", "ac")
        ctx.store(outv[:, oc, tt * TT:(tt + 1) * TT], stgO[:, par, :])

    emit_proj_stage(ctx, inname, FF, FF, [wname], 1024, ev)


@stage("fh1", ["h_bf", "fh1T", "fsh1"], ["m1_bf"])
def emit_fh1(ctx):
    _gelu_stage(ctx, "h_bf", "fh1T", "fsh1", "m1_bf")


@stage("fh2", ["m1_bf", "fh2T", "fsh2"], ["m2_bf"])
def emit_fh2(ctx):
    _gelu_stage(ctx, "m1_bf", "fh2T", "fsh2", "m2_bf")


@stage("w3c", ["m2_bf", "fw3T", "fs3", "s0", "x_fm"], ["x1"])
def emit_w3c(ctx):
    s3 = load_scale(ctx, "fs3")
    s0_t = ctx.sbuf("s0t", (128, NC, TT), F32)
    x_t = ctx.sbuf("xt", (128, NC, TT), F32)
    stgA = ctx.sbuf("stgA", (128, 2, TT), F32)
    stgO = ctx.sbuf("stgO", (128, 2, TT), F32)
    outv = fm_view(ctx.aps["x1"], NC)

    def pre_tt(tt):
        ctx.wait("sync", "ve")
        ctx.dma("sync", s0_t[:], fm_view(ctx.aps["s0"], NC)[:, :, tt * TT:(tt + 1) * TT])
        ctx.dma("sync", x_t[:], fm_view(ctx.aps["x_fm"], NC)[:, :, tt * TT:(tt + 1) * TT])

    def ev(oc, tt, pps, par):
        ctx.wait("scalar", "mm")
        ctx.wait("scalar", "ve")
        ctx.run("scalar", lambda e, par=par, oc=oc: e.activation(
            stgA[:, par, :], pps[0][:], AF.Silu, scale=s3[:, oc:oc + 1]), inc="ac")
        ctx.wait("vector", "ac")
        ctx.wait("vector", "ld")
        ctx.wait("vector", "st")
        ctx.run("vector", lambda e, par=par, oc=oc: e.tensor_tensor(
            stgA[:, par, :], stgA[:, par, :], s0_t[:, oc, :], ALU.mult))
        ctx.run("vector", lambda e, par=par, oc=oc: e.tensor_tensor(
            stgO[:, par, :], stgA[:, par, :], x_t[:, oc, :], ALU.add), inc="ve")
        ctx.wait("sync", "ve")
        ctx.store(outv[:, oc, tt * TT:(tt + 1) * TT], stgO[:, par, :])

    emit_proj_stage(ctx, "m2_bf", FF, D, ["fw3T"], D, ev, pre_tt=pre_tt)


@stage("ln1", ["x1"], ["xn1f", "xn1_bf"])
def emit_ln1(ctx):
    x_sb = ctx.sbuf("x", (128, NC, TL), F32)
    ctx.dma("sync", x_sb[:], fm_view(ctx.aps["x1"], NC))
    ones_f = emit_ones(ctx, F32, (128, 1))
    onesr_f = emit_ones(ctx, F32, (1, 128))
    sq = ctx.sbuf("sq", (128, NC, TT), F32)
    xn = ctx.sbuf("xn", (128, NC, TT), F32)
    xnb = ctx.sbuf("xnb", (128, NC, TT), BF16)
    rows = ctx.sbuf("rows", (1, 8, TT), F32)
    psums = (ctx.psum("sum", (1, TT), F32), ctx.psum("sq", (1, TT), F32),
             ctx.psum("bcr", (128, TT), F32), ctx.psum("bcn", (128, TT), F32))
    outf = fm_view(ctx.aps["xn1f"], NC)
    outb = fm_view(ctx.aps["xn1_bf"], NC)
    for tt in range(NTT):
        sl = slice(tt * TT, (tt + 1) * TT)

        def apply(c, bcr, bcn, sl=sl):
            ctx.run("vector", lambda e, c=c: e.tensor_tensor(xn[:, c, :], x_sb[:, c, sl], bcr[:, :], ALU.mult))
            ctx.run("vector", lambda e, c=c: e.tensor_tensor(xn[:, c, :], xn[:, c, :], bcn[:, :], ALU.add))
            ctx.run("vector", lambda e, c=c: e.tensor_copy(xnb[:, c, :], xn[:, c, :]), inc="ve")

        ve = emit_layernorm(ctx, lambda c, sl=sl: x_sb[:, c, sl], TT, apply,
                            ones_f, onesr_f, psums, rows, sq)
        ctx.wait("sync", "ve", ve)
        ctx.store(outf[:, :, sl], xn[:])
        st = ctx.store(outb[:, :, sl], xnb[:])
        ctx.wait("vector", "st", st)
        ctx.wait("scalar", "st", st)
        ctx.wait("tensor", "st", st)


@stage("conv_a", ["xn1f", "ln1w", "ln1b", "convw", "convb", "halosel"],
       ["xc_bf", "xch_bf"], cc=("cc_h_in", "cc_h_out"))
def emit_conv_a(ctx):
    nc = ctx.nc
    ccsem = nc.semaphore("sem_cch").__enter__()
    hext = ctx.sbuf("hext", (128, NC, 4 + TL), F32)
    xn_t = ctx.sbuf("xnt", (128, NC, TT), F32)
    lw = load_scale(ctx, "ln1w")
    lb = load_scale(ctx, "ln1b")
    cw = load_scale(ctx, "convw")
    cb = load_scale(ctx, "convb")
    hsel = load_scale(ctx, "halosel")
    ones_f = emit_ones(ctx, F32, (128, 1))
    onesr_f = emit_ones(ctx, F32, (1, 128))
    sq = ctx.sbuf("sq", (128, NC, TT), F32)
    rows = ctx.sbuf("rows", (1, 8, TT), F32)
    psums = (ctx.psum("sum", (1, TT), F32), ctx.psum("sq", (1, TT), F32),
             ctx.psum("bcr", (128, TT), F32), ctx.psum("bcn", (128, TT), F32))

    for tt in range(NTT):
        sl = slice(tt * TT, (tt + 1) * TT)
        ctx.wait("sync", "mm")
        ctx.wait("sync", "ac")
        ctx.wait("sync", "ve")
        ctx.dma("sync", xn_t[:], fm_view(ctx.aps["xn1f"], NC)[:, :, sl])

        def apply(c, bcr, bcn, tt=tt):
            ctx.run("vector", lambda e, c=c: e.tensor_tensor(sq[:, c, :], xn_t[:, c, :], bcr[:, :], ALU.mult))
            ctx.run("vector", lambda e, c=c: e.tensor_tensor(sq[:, c, :], sq[:, c, :], bcn[:, :], ALU.add))
            ctx.run("vector", lambda e, c=c, tt=tt: e.tensor_scalar(
                hext[:, c, 4 + tt * TT:4 + (tt + 1) * TT], sq[:, c, :],
                lw[:, c:c + 1], lb[:, c:c + 1], ALU.mult, ALU.add), inc="ve")

        emit_layernorm(ctx, lambda c: xn_t[:, c, :], TT, apply,
                       ones_f, onesr_f, psums, rows, sq)

    ctx.wait("sync", "ve")
    st_h = ctx.store(ctx.aps["cc_h_in"].rearrange("(c p) t -> p c t", p=128),
                     hext[:, :, TL:TL + 4])
    ctx.wait("gpsimd", "st", st_h)
    ctx.rec("gpsimd", lambda e, s=ccsem: e.collective_compute(
        "AllGather", ALU.bypass, replica_groups=GROUPS,
        ins=[ctx.aps["cc_h_in"].opt()], outs=[ctx.aps["cc_h_out"].opt()]).then_inc(s))
    ctx.rec("sync", lambda e, s=ccsem: e.wait_ge(s, 1))
    gat = ctx.sbuf("gat", (128, 4, NC, 4), F32)
    ctx.dma("sync", gat[:], ctx.aps["cc_h_out"].rearrange("(r c p) t -> p r c t", p=128, c=NC))
    ld_g = ctx.cnt["ld"]
    ctx.wait("vector", "ld", ld_g)
    haloA = ctx.sbuf("haloA", (128, NC, 4), F32)
    haloB = ctx.sbuf("haloB", (128, NC, 4), F32)
    ctx.run("vector", lambda e: e.memset(haloA[:], 0.0))
    pa, pb = haloA, haloB
    for r in range(4):
        ctx.run("vector", lambda e, r=r, pa=pa, pb=pb: e.scalar_tensor_tensor(
            pb[:], gat[:, r, :, :], hsel[:, r:r + 1], pa[:], ALU.mult, ALU.add))
        pa, pb = pb, pa
    ctx.run("vector", lambda e, pa=pa: e.tensor_copy(hext[:, :, 0:4], pa[:]), inc="ve")

    ca = ctx.sbuf("ca", (128, TT + 1), F32)
    cbuf = ctx.sbuf("cb2", (128, TT + 1), F32)
    xcs = ctx.sbuf("xcs", (128, NC, TT + 1), BF16)
    xch = ctx.sbuf("xch", (128, NC, 2), BF16)
    ctx.run("vector", lambda e: e.memset(xch[:], 0.0), inc="ve")
    outv = ctx.aps["xc_bf"].rearrange("(c p) t -> p c t", p=128)
    for blk in range(NTT):
        i0 = 0 if blk == 0 else blk * TT + 1
        W = TT + 1 if blk == 0 else TT
        ctx.wait("vector", "st")
        for c in range(NC):
            ctx.run("vector", lambda e, c=c, i0=i0, W=W: e.tensor_scalar(
                ca[:, 0:W], hext[:, c, i0:i0 + W], cw[:, c, 0:1], None, ALU.mult))
            ctx.run("vector", lambda e, c=c, i0=i0, W=W: e.scalar_tensor_tensor(
                cbuf[:, 0:W], hext[:, c, i0 + 1:i0 + 1 + W], cw[:, c, 1:2], ca[:, 0:W], ALU.mult, ALU.add))
            ctx.run("vector", lambda e, c=c, i0=i0, W=W: e.scalar_tensor_tensor(
                ca[:, 0:W], hext[:, c, i0 + 2:i0 + 2 + W], cw[:, c, 2:3], cbuf[:, 0:W], ALU.mult, ALU.add))
            ctx.run("vector", lambda e, c=c, i0=i0, W=W: e.scalar_tensor_tensor(
                cbuf[:, 0:W], hext[:, c, i0 + 3:i0 + 3 + W], cw[:, c, 3:4], ca[:, 0:W], ALU.mult, ALU.add))
            ctx.run("vector", lambda e, c=c, W=W: e.tensor_scalar(
                xcs[:, c, 0:W], cbuf[:, 0:W], cb[:, c:c + 1], None, ALU.add), inc="ve")
            if blk == 0:
                ctx.run("vector", lambda e, c=c: e.tensor_scalar(
                    xch[:, c, 0:1], cbuf[:, 0:1], cb[:, c:c + 1], None, ALU.add), inc="ve")
        ctx.wait("sync", "ve")
        if blk == 0:
            ctx.store(outv[:, :, 0:TT], xcs[:, :, 1:TT + 1])
            ctx.store(ctx.aps["xch_bf"], xch[:])
        else:
            ctx.store(outv[:, :, i0 - 1:i0 - 1 + W], xcs[:, :, 0:W])


@stage("conv_b", ["xc_bf", "xch_bf", "qkT", "vT", "gT", "gb", "cos_fm", "t16", "gammaA",
                  "ind_bc", "ind_sq", "rankmask", "haloflag"],
       ["sdt", "qL1", "carry", "dbg1", "dbg2"], cc=("cc_sd_in", "cc_sd_out"))
def emit_conv_b(ctx):
    import os
    PHASE = int(os.environ.get("KCONVB_PHASE", "99"))
    TTC = 256
    NTTC = TL // TTC

    nc = ctx.nc
    ccsem = nc.semaphore("sem_ccsd").__enter__()
    xc_halo = ctx.sbuf("xch", (128, NC, 2), BF16)
    xc_sb = ctx.sbuf("xc", (128, NC, TL), BF16)
    hflag = ctx.sbuf("hflag", (128, 1), F32)
    ctx.dma("sync", xc_halo[:], ctx.aps["xch_bf"].rearrange("p (c two) -> p c two", two=2))
    ctx.dma("sync", xc_sb[:], ctx.aps["xc_bf"].rearrange("(c p) t -> p c t", p=128))
    ctx.dma("sync", hflag[:], ctx.aps["haloflag"])
    ctx.wait("scalar", "ld")
    ctx.run("scalar", lambda e: e.activation(xc_halo[:], xc_halo[:], AF.Copy, scale=hflag[:, 0:1]), inc="ac")
    wq = load_weight_cache(ctx, "qkT", D, D)
    wv = load_weight_cache(ctx, "vT", D, D)
    wg = load_weight_cache(ctx, "gT", D, D)
    gb_sb = load_scale(ctx, "gb")
    rkm = load_scale(ctx, "rankmask")
    gam = ctx.sbuf("gam", (16, 1), F32)
    ctx.dma("sync", gam[:], ctx.aps["gammaA"])
    t16_sb = ctx.sbuf("t16", (16, TL), F32)
    ctx.dma("sync", t16_sb[:], ctx.aps["t16"])
    indbc = ctx.sbuf("indbc", (16, NC, 128), F32)
    ctx.dma("sync", indbc[:], ctx.aps["ind_bc"].rearrange("m (c p) -> m c p", p=128))
    indsq = ctx.sbuf("indsq", (128, NC, 16), BF16)
    ctx.dma("sync", indsq[:], ctx.aps["ind_sq"].rearrange("p (c m) -> p c m", m=16))
    ld0 = ctx.cnt["ld"]

    lng = ctx.sbuf("lng", (16, 2), F32)
    dec = ctx.sbuf("dec", (16, TL), F32)
    ctx.wait("scalar", "ld", ld0)
    ctx.run("scalar", lambda e: e.activation(lng[:, 0:1], gam[:], AF.Ln), inc="ac")
    ctx.wait("vector", "ac")
    ctx.run("vector", lambda e: e.tensor_scalar_max(lng[:, 0:1], lng[:, 0:1], -3e37), inc="ve")
    ctx.wait("scalar", "ve")
    ctx.run("scalar", lambda e: e.copy(lng[:, 1:2], lng[:, 0:1]), inc="ac")
    ctx.wait("vector", "ac")
    ctx.run("vector", lambda e: e.tensor_scalar(dec[:], t16_sb[:], lng[:, 1:2], None, ALU.mult), inc="ve")
    ctx.wait("scalar", "ve")
    ctx.run("scalar", lambda e: e.activation(dec[:], dec[:], AF.Exp), inc="ac")
    ac_dec = ctx.cnt["ac"]

    ps_qk = [ctx.psum(f"qk{i}", (128, TTC), F32) for i in range(2)]
    ps_vg = [ctx.psum(f"vg{i}", (128, TTC), F32) for i in range(2)]
    ps_ss = ctx.psum("ss", (16, TTC), F32)
    ps_bcn = ctx.psum("bcn", (128, TTC), F32)
    ps_dec = ctx.psum("dec", (128, TTC), F32)
    qk_sb = ctx.sbuf("qks", (128, NC, TTC), F32)
    sq_bf = ctx.sbuf("sqb", (128, NC, TTC), BF16)
    rn = ctx.sbuf("rn", (16, 3, TTC), F32)
    k_t = ctx.sbuf("kt", (128, NC, TTC + 1), F32)
    v_sb = ctx.sbuf("vs", (128, NC, TTC), F32)
    g_sb = ctx.sbuf("gs", (128, NC, TTC), F32)
    cos_t = ctx.sbuf("cost", (128, NC, TTC), F32)
    stA = ctx.sbuf("stA", (128, TTC), F32)
    stB = ctx.sbuf("stB", (128, TTC), F32)
    sd_sb = ctx.sbuf("sds", (128, NC, TTC), F32)
    redbuf = ctx.sbuf("redbuf", (128, NTTC, NC), F32)
    redbuf2 = ctx.sbuf("redbuf2", (128, NTTC, NC), F32)
    final = ctx.sbuf("final", (128, NC), F32)
    ctx.wait("tensor", "ld", ld0)
    ctx.wait("tensor", "ac", ac_dec)
    ctx.wait("tensor", "ve")

    SUBPH = int(os.environ.get("KCONVB_SUB", "9"))

    NOC = int(os.environ.get("KCONVB_OC", "8"))
    NEV = int(os.environ.get("KCONVB_EVAC", "2"))

    def qk_norm_pass(src_fn, colw, dst_off):
        for oc in range(NOC):
            pp = ps_qk[oc % 2]
            ctx.wait("tensor", "ve")
            ctx.wait("tensor", "ac")
            for kcc in range(NC):
                ctx.run("tensor", lambda e, oc=oc, kcc=kcc, pp=pp, colw=colw: e.matmul(
                    pp[:, 0:colw], wq[:, kcc, oc * 128:(oc + 1) * 128],
                    src_fn(kcc),
                    start=(kcc == 0), stop=(kcc == NC - 1)),
                    inc="mm" if kcc == NC - 1 else None)
            if NEV >= 1:
                ctx.wait("vector", "mm")
                ctx.run("vector", lambda e, oc=oc, pp=pp, colw=colw: e.tensor_copy(
                    qk_sb[:, oc, 0:colw], pp[:, 0:colw]), inc="ve")
            if NEV >= 2:
                ctx.wait("scalar", "ve")
                ctx.run("scalar", lambda e, oc=oc, colw=colw: e.activation(
                    sq_bf[:, oc, 0:colw], qk_sb[:, oc, 0:colw], AF.Square), inc="ac")
        if SUBPH < 2:
            return
        ctx.wait("tensor", "ac")
        ctx.wait("tensor", "ve")
        for c in range(NC):
            ctx.run("tensor", lambda e, c=c, colw=colw: e.matmul(
                ps_ss[:, 0:colw], indsq[:, c, :], sq_bf[:, c, 0:colw],
                start=(c == 0), stop=(c == NC - 1)), inc="mm" if c == NC - 1 else None)
        if SUBPH < 3:
            return
        ctx.wait("scalar", "mm")
        ctx.run("scalar", lambda e, colw=colw: e.activation(rn[:, 0, 0:colw], ps_ss[:, 0:colw], AF.Sqrt), inc="ac")
        ctx.wait("vector", "ac")
        ctx.run("vector", lambda e, colw=colw: e.tensor_scalar_max(rn[:, 0, 0:colw], rn[:, 0, 0:colw], 1e-12), inc="ve")
        ctx.wait("scalar", "ve")
        ctx.run("scalar", lambda e, colw=colw: e.copy(rn[:, 2, 0:colw], rn[:, 0, 0:colw]), inc="ac")
        ctx.wait("vector", "ac")
        ctx.run("vector", lambda e, colw=colw: e.reciprocal(rn[:, 1, 0:colw], rn[:, 2, 0:colw]), inc="ve")
        if SUBPH < 4:
            return
        for c in range(NC):
            ctx.wait("tensor", "ve")
            ctx.run("tensor", lambda e, c=c, colw=colw: e.matmul(
                ps_bcn[:, 0:colw], indbc[:, c, :], rn[:, 1, 0:colw], start=True, stop=True), inc="mm")
            ctx.wait("vector", "mm")
            ctx.run("vector", lambda e, c=c, colw=colw, dst_off=dst_off: e.tensor_tensor(
                k_t[:, c, dst_off:dst_off + colw], qk_sb[:, c, 0:colw],
                ps_bcn[:, 0:colw], ALU.mult), inc="ve")

    if PHASE < 2:
        return
    qk_norm_pass(lambda kcc: xc_halo[:, kcc, 0:1], 1, 0)
    if PHASE < 3:
        return

    sdtv = fm_view(ctx.aps["sdt"], NC)
    qv = fm_view(ctx.aps["qL1"], NC)
    for tt in range(NTTC):
        if tt > 0:
            ctx.wait("scalar", "st")
            ctx.wait("scalar", "ve")
            ctx.run("scalar", lambda e: e.copy(k_t[:, :, 0:1], k_t[:, :, TTC:TTC + 1]), inc="ac")
        qk_norm_pass(lambda kcc, tt=tt: xc_sb[:, kcc, tt * TTC:(tt + 1) * TTC], TTC, 1)
        if PHASE < 4:
            return
        ctx.wait("sync", "ve")
        ctx.dma("sync", cos_t[:], ctx.aps["cos_fm"][:, :, tt * TTC:(tt + 1) * TTC])
        ld_cos = ctx.cnt["ld"]
        for oc in range(NC):
            pv, pg = ps_vg[0], ps_vg[1]
            ctx.wait("tensor", "ve")
            ctx.wait("tensor", "ac")
            for kcc in range(NC):
                ctx.run("tensor", lambda e, oc=oc, kcc=kcc, pv=pv, tt=tt: e.matmul(
                    pv[:], wv[:, kcc, oc * 128:(oc + 1) * 128],
                    xc_sb[:, kcc, tt * TTC:(tt + 1) * TTC],
                    start=(kcc == 0), stop=(kcc == NC - 1)))
                ctx.run("tensor", lambda e, oc=oc, kcc=kcc, pg=pg, tt=tt: e.matmul(
                    pg[:], wg[:, kcc, oc * 128:(oc + 1) * 128],
                    xc_sb[:, kcc, tt * TTC:(tt + 1) * TTC],
                    start=(kcc == 0), stop=(kcc == NC - 1)),
                    inc="mm" if kcc == NC - 1 else None)
            ctx.wait("vector", "mm")
            ctx.run("vector", lambda e, oc=oc, pv=pv: e.tensor_copy(v_sb[:, oc, :], pv[:]), inc="ve")
            ctx.wait("scalar", "mm")
            ctx.run("scalar", lambda e, oc=oc, pg=pg: e.activation(
                g_sb[:, oc, :], pg[:], AF.Sigmoid, bias=gb_sb[:, oc:oc + 1]), inc="ac")
        if PHASE < 5:
            return
        ctx.wait("vector", "ld", ld_cos)
        ctx.wait("vector", "ac")
        for c in range(NC):
            ctx.wait("tensor", "ve")
            ctx.run("tensor", lambda e, c=c, tt=tt: e.matmul(
                ps_dec[:], indbc[:, c, :], dec[:, tt * TTC:(tt + 1) * TTC], start=True, stop=True), inc="mm")
            ctx.run("vector", lambda e, c=c: e.tensor_tensor(stA[:], v_sb[:, c, :], cos_t[:, c, :], ALU.mult))
            ctx.run("vector", lambda e, c=c: e.tensor_tensor(stB[:], v_sb[:, c, :], k_t[:, c, 0:TTC], ALU.mult))
            ctx.run("vector", lambda e, c=c: e.tensor_tensor(stA[:], stA[:], stB[:], ALU.add))
            ctx.run("vector", lambda e, c=c: e.tensor_tensor(stA[:], stA[:], g_sb[:, c, :], ALU.mult))
            ctx.wait("vector", "mm")
            ctx.run("vector", lambda e, c=c: e.tensor_tensor(sd_sb[:, c, :], stA[:], ps_dec[:], ALU.mult))
            ctx.run("vector", lambda e, c=c, tt=tt: e.reduce_sum(
                redbuf[:, tt, c:c + 1], sd_sb[:, c, :], axis=AX), inc="ve")
        ctx.wait("sync", "ve")
        ctx.store(sdtv[:, :, tt * TTC:(tt + 1) * TTC], sd_sb[:])
        ctx.store(qv[:, :, tt * TTC:(tt + 1) * TTC], k_t[:, :, 1:TTC + 1])

    if PHASE < 6:
        return
    ctx.wait("scalar", "ve")
    ctx.run("scalar", lambda e: e.copy(redbuf2[:], redbuf[:]), inc="ac")
    ctx.wait("vector", "ac")
    for c in range(NC):
        ctx.run("vector", lambda e, c=c: e.reduce_sum(
            final[:, c:c + 1], redbuf2[:, :, c], axis=AX), inc="ve" if c == NC - 1 else None)
    ctx.wait("sync", "ve")
    st_sd = ctx.store(ctx.aps["cc_sd_in"], final[:])
    ctx.wait("gpsimd", "st", st_sd)
    ctx.rec("gpsimd", lambda e, s=ccsem: e.collective_compute(
        "AllGather", ALU.bypass, replica_groups=GROUPS,
        ins=[ctx.aps["cc_sd_in"].opt()], outs=[ctx.aps["cc_sd_out"].opt()]).then_inc(s))
    ctx.rec("sync", lambda e, s=ccsem: e.wait_ge(s, 1))
    ctx.store(ctx.aps["dbg1"], final[:])
    gat = ctx.sbuf("gat", (128, 4, NC), F32)
    ctx.dma("sync", gat[:], ctx.aps["cc_sd_out"].rearrange("(r p) c -> p r c", p=128))
    ld_g = ctx.cnt["ld"]
    ctx.wait("sync", "ld", ld_g)
    ctx.store(ctx.aps["dbg2"], gat[:].rearrange("p r c -> p (r c)"))
    ctx.wait("vector", "ld", ld_g)
    prefA = ctx.sbuf("prefA", (128, NC), F32)
    prefB = ctx.sbuf("prefB", (128, NC), F32)
    ctx.run("vector", lambda e: e.memset(prefA[:], 0.0))
    pa, pb = prefA, prefB
    for r in range(4):
        ctx.run("vector", lambda e, r=r, pa=pa, pb=pb: e.scalar_tensor_tensor(
            pb[:], gat[:, r, :], rkm[:, r:r + 1], pa[:], ALU.mult, ALU.add),
            inc="ve" if r == 3 else None)
        pa, pb = pb, pa
    ctx.wait("sync", "ve")
    ctx.store(ctx.aps["carry"], pa[:])


@stage("l1scan", ["sdt", "carry", "qL1", "gammaA", "t16", "ind_bc"], ["at1_bf"])
def emit_l1scan(ctx):
    sd_sb = ctx.sbuf("sd", (128, NC, TL), F32)
    q_sb = ctx.sbuf("q", (128, NC, TL), F32)
    carry = ctx.sbuf("carry", (128, NC), F32)
    gam = ctx.sbuf("gam", (16, 1), F32)
    t16_sb = ctx.sbuf("t16", (16, TL), F32)
    indbc = ctx.sbuf("indbc", (16, NC, 128), F32)
    ctx.dma("sync", sd_sb[:], fm_view(ctx.aps["sdt"], NC))
    ctx.dma("sync", q_sb[:], fm_view(ctx.aps["qL1"], NC))
    ctx.dma("sync", carry[:], ctx.aps["carry"])
    ctx.dma("sync", gam[:], ctx.aps["gammaA"])
    ctx.dma("sync", t16_sb[:], ctx.aps["t16"])
    ctx.dma("sync", indbc[:], ctx.aps["ind_bc"].rearrange("m (c p) -> m c p", p=128))
    ld0 = ctx.cnt["ld"]

    lng = ctx.sbuf("lng", (16, 2), F32)
    rd = ctx.sbuf("rd", (16, TL), F32)
    ctx.wait("scalar", "ld", ld0)
    ctx.run("scalar", lambda e: e.activation(lng[:, 0:1], gam[:], AF.Ln), inc="ac")
    ctx.wait("vector", "ac")
    ctx.run("vector", lambda e: e.tensor_scalar_max(lng[:, 0:1], lng[:, 0:1], -3e37), inc="ve")
    ctx.wait("scalar", "ve")
    ctx.run("scalar", lambda e: e.copy(lng[:, 1:2], lng[:, 0:1]), inc="ac")
    ctx.wait("vector", "ac")
    ctx.run("vector", lambda e: e.tensor_scalar(rd[:], t16_sb[:], lng[:, 1:2], None, ALU.mult), inc="ve")
    ctx.wait("scalar", "ve")
    ctx.run("scalar", lambda e: e.activation(rd[:], rd[:], AF.Exp), inc="ac")
    ctx.wait("vector", "ac")
    ctx.run("vector", lambda e: e.tensor_scalar_add(rd[:], rd[:], 1e-6))
    ctx.run("vector", lambda e: e.reciprocal(rd[:], rd[:]), inc="ve")

    zeros = ctx.sbuf("zeros", (128, TL), F32)
    M = ctx.sbuf("M", (128, TL), F32)
    ostage = ctx.sbuf("ost", (128, TL), BF16)
    ps_rd = ctx.psum("rd", (128, TT), F32)
    ctx.run("vector", lambda e: e.memset(zeros[:], 0.0), inc="ve")
    outv = fm_view(ctx.aps["at1_bf"], NC)
    ctx.wait("vector", "ld", ld0)
    ctx.wait("tensor", "ld", ld0)
    for c in range(NC):
        ctx.wait("vector", "st")
        ctx.run("vector", lambda e, c=c: e.tensor_tensor_scan(
            M[:], sd_sb[:, c, :], zeros[:], carry[:, c:c + 1], ALU.add, ALU.add), inc="ve")
        for tt in range(NTT):
            ctx.wait("tensor", "ve")
            ctx.run("tensor", lambda e, c=c, tt=tt: e.matmul(
                ps_rd[:], indbc[:, c, :], rd[:, tt * TT:(tt + 1) * TT], start=True, stop=True), inc="mm")
            ctx.wait("vector", "mm")
            ctx.run("vector", lambda e, c=c, tt=tt: e.tensor_tensor(
                M[:, tt * TT:(tt + 1) * TT], M[:, tt * TT:(tt + 1) * TT], ps_rd[:], ALU.mult))
            ctx.run("vector", lambda e, c=c, tt=tt: e.tensor_tensor(
                ostage[:, tt * TT:(tt + 1) * TT], M[:, tt * TT:(tt + 1) * TT],
                q_sb[:, c, tt * TT:(tt + 1) * TT], ALU.mult), inc="ve")
        ctx.wait("sync", "ve")
        ctx.store(outv[:, c, :], ostage[:])


@stage("oproj", ["at1_bf", "oT", "onw", "onb", "xn1f", "ln2w", "ln2b", "gamma1"],
       ["xmid", "h2_bf"])
def emit_oproj(ctx):
    at_sb = ctx.sbuf("at", (128, NC, TL), BF16)
    ctx.dma("sync", at_sb[:], fm_view(ctx.aps["at1_bf"], NC))
    wo = load_weight_cache(ctx, "oT", D, D)
    onw = load_scale(ctx, "onw")
    onb = load_scale(ctx, "onb")
    l2w = load_scale(ctx, "ln2w")
    l2b = load_scale(ctx, "ln2b")
    g1 = load_scale(ctx, "gamma1")
    xn1_t = ctx.sbuf("xn1t", (128, NC, TT), F32)
    ones_f = emit_ones(ctx, F32, (128, 1))
    onesr_f = emit_ones(ctx, F32, (1, 128))
    sq = ctx.sbuf("sq", (128, NC, TT), F32)
    rows = ctx.sbuf("rows", (1, 8, TT), F32)
    psums = (ctx.psum("sum", (1, TT), F32), ctx.psum("sq2", (1, TT), F32),
             ctx.psum("bcr", (128, TT), F32), ctx.psum("bcn", (128, TT), F32))
    ps_mm = [ctx.psum(f"mm{i}", (128, TT), F32) for i in range(2)]
    ao = ctx.sbuf("ao", (128, NC, TT), F32)
    xmid_sb = ctx.sbuf("xm", (128, NC, TT), F32)
    h2_sb = ctx.sbuf("h2", (128, NC, TT), BF16)
    xmv = fm_view(ctx.aps["xmid"], NC)
    h2v = fm_view(ctx.aps["h2_bf"], NC)
    ld0 = ctx.cnt["ld"]
    ctx.wait("tensor", "ld", ld0)
    for tt in range(NTT):
        sl = slice(tt * TT, (tt + 1) * TT)
        ctx.wait("sync", "mm")
        ctx.wait("sync", "ac")
        ctx.wait("sync", "ve")
        ctx.dma("sync", xn1_t[:], fm_view(ctx.aps["xn1f"], NC)[:, :, sl])
        ld_x = ctx.cnt["ld"]
        for oc in range(NC):
            pp = ps_mm[oc % 2]
            ctx.wait("tensor", "ve")
            ctx.wait("tensor", "ac")
            for kcc in range(NC):
                ctx.run("tensor", lambda e, oc=oc, kcc=kcc, pp=pp, sl=sl: e.matmul(
                    pp[:], wo[:, kcc, oc * 128:(oc + 1) * 128], at_sb[:, kcc, sl],
                    start=(kcc == 0), stop=(kcc == NC - 1)),
                    inc="mm" if kcc == NC - 1 else None)
            ctx.wait("vector", "mm")
            ctx.run("vector", lambda e, oc=oc, pp=pp: e.tensor_copy(ao[:, oc, :], pp[:]), inc="ve")

        ctx.wait("vector", "ld", ld_x)

        def apply1(c, bcr, bcn):
            ctx.run("vector", lambda e, c=c: e.tensor_tensor(sq[:, c, :], ao[:, c, :], bcr[:, :], ALU.mult))
            ctx.run("vector", lambda e, c=c: e.tensor_tensor(sq[:, c, :], sq[:, c, :], bcn[:, :], ALU.add))
            ctx.run("vector", lambda e, c=c: e.tensor_scalar(
                sq[:, c, :], sq[:, c, :], onw[:, c:c + 1], onb[:, c:c + 1], ALU.mult, ALU.add))
            ctx.run("vector", lambda e, c=c: e.scalar_tensor_tensor(
                xmid_sb[:, c, :], sq[:, c, :], g1[:, c:c + 1], xn1_t[:, c, :], ALU.mult, ALU.add), inc="ve")

        emit_layernorm(ctx, lambda c: ao[:, c, :], TT, apply1,
                       ones_f, onesr_f, psums, rows, sq)

        def apply2(c, bcr, bcn):
            ctx.run("vector", lambda e, c=c: e.tensor_tensor(sq[:, c, :], xmid_sb[:, c, :], bcr[:, :], ALU.mult))
            ctx.run("vector", lambda e, c=c: e.tensor_tensor(sq[:, c, :], sq[:, c, :], bcn[:, :], ALU.add))
            ctx.run("vector", lambda e, c=c: e.tensor_scalar(
                h2_sb[:, c, :], sq[:, c, :], l2w[:, c:c + 1], l2b[:, c:c + 1], ALU.mult, ALU.add), inc="ve")

        emit_layernorm(ctx, lambda c: xmid_sb[:, c, :], TT, apply2,
                       ones_f, onesr_f, psums, rows, sq)
        ctx.wait("sync", "ve")
        ctx.store(xmv[:, :, sl], xmid_sb[:])
        st = ctx.store(h2v[:, :, sl], h2_sb[:])
        ctx.wait("vector", "st", st)
        ctx.wait("scalar", "st", st)
        ctx.wait("tensor", "st", st)


@stage("gu", ["h2_bf", "gateT", "upT"], ["m_bf"])
def emit_gu(ctx):
    stgA = ctx.sbuf("stgA", (128, 2, TT), F32)
    stgO = ctx.sbuf("stgO", (128, 2, TT), BF16)
    outv = fm_view(ctx.aps["m_bf"], FF // 128)

    def ev(oc, tt, pps, par):
        ctx.wait("scalar", "mm")
        ctx.wait("scalar", "ve")
        ctx.run("scalar", lambda e, par=par: e.activation(stgA[:, par, :], pps[0][:], AF.Silu), inc="ac")
        ctx.wait("vector", "ac")
        ctx.wait("vector", "st")
        ctx.run("vector", lambda e, par=par: e.tensor_tensor(
            stgO[:, par, :], stgA[:, par, :], pps[1][:], ALU.mult), inc="ve")
        ctx.wait("sync", "ve")
        ctx.store(outv[:, oc, tt * TT:(tt + 1) * TT], stgO[:, par, :])

    emit_proj_stage(ctx, "h2_bf", D, FF, ["gateT", "upT"], 2048, ev)


@stage("down", ["m_bf", "downT", "xmid", "gamma2"], ["sl1"])
def emit_down(ctx):
    g2 = load_scale(ctx, "gamma2")
    xm_t = ctx.sbuf("xmt", (128, NC, TT), F32)
    stgO = ctx.sbuf("stgO", (128, 2, TT), F32)
    outv = fm_view(ctx.aps["sl1"], NC)

    def pre_tt(tt):
        ctx.wait("sync", "ve")
        ctx.dma("sync", xm_t[:], fm_view(ctx.aps["xmid"], NC)[:, :, tt * TT:(tt + 1) * TT])

    def ev(oc, tt, pps, par):
        ctx.wait("vector", "mm")
        ctx.wait("vector", "st")
        ctx.wait("vector", "ld")
        ctx.run("vector", lambda e, par=par, oc=oc: e.scalar_tensor_tensor(
            stgO[:, par, :], pps[0][:], g2[:, oc:oc + 1], xm_t[:, oc, :], ALU.mult, ALU.add), inc="ve")
        ctx.wait("sync", "ve")
        ctx.store(outv[:, oc, tt * TT:(tt + 1) * TT], stgO[:, par, :])

    emit_proj_stage(ctx, "m_bf", FF, D, ["downT"], D, ev, pre_tt=pre_tt)


@stage("router", ["xn1f", "routT", "ident"], ["comb", "dbg3", "dbg4", "dbg5"])
def emit_router(ctx):
    rt = ctx.sbuf("rt", (128, NC, NE), F32)
    ctx.dma("sync", rt[:], ctx.aps["routT"].rearrange("(c p) o -> p c o", p=128))
    idt = ctx.sbuf("idt", (128, 128), F32)
    ctx.dma("sync", idt[:], ctx.aps["ident"])
    xn1_t = ctx.sbuf("xn1t", (128, NC, TT), F32)
    ps_lg = ctx.psum("lg", (NE, TT), F32)
    ps_t1 = ctx.psum("t1", (128, 128), F32)
    ps_t2 = ctx.psum("t2", (NE, 128), F32)
    lg = ctx.sbuf("lg", (NE, TT), F32)
    ltm = ctx.sbuf("ltm", (128, NE), F32)
    l2 = ctx.sbuf("l2", (128, NE), F32)
    m1 = ctx.sbuf("m1", (128, 4), F32)
    bounce = ctx.sbuf("bounce", (128, 4), F32)
    w12 = ctx.sbuf("w12", (128, 4), F32)
    cstage = ctx.sbuf("cst", (128, NE), F32)
    ltm2 = ctx.sbuf("ltm2", (128, NE), F32)
    cst2 = ctx.sbuf("cst2", (128, NE), F32)
    l22 = ctx.sbuf("l22", (128, NE), F32)
    c1m = ctx.sbuf("c1m", (128, NE), F32)
    c1m2 = ctx.sbuf("c1m2", (128, NE), F32)
    m2m = ctx.sbuf("m2m", (128, NE), F32)
    m2m2 = ctx.sbuf("m2m2", (128, NE), F32)
    cst3 = ctx.sbuf("cst3", (128, NE), F32)
    cfm = ctx.sbuf("cfm", (NE, 128), F32)
    outv = ctx.aps["comb"]
    ld0 = ctx.cnt["ld"]
    ctx.wait("tensor", "ld", ld0)
    for tt in range(NTT):
        sl = slice(tt * TT, (tt + 1) * TT)
        ctx.wait("sync", "mm")
        ctx.dma("sync", xn1_t[:], fm_view(ctx.aps["xn1f"], NC)[:, :, sl])
        ld_x = ctx.cnt["ld"]
        ctx.wait("tensor", "ld", ld_x)
        ctx.wait("tensor", "ve")
        for kcc in range(NC):
            ctx.run("tensor", lambda e, kcc=kcc: e.matmul(
                ps_lg[:], rt[:, kcc, :], xn1_t[:, kcc, :],
                start=(kcc == 0), stop=(kcc == NC - 1)),
                inc="mm" if kcc == NC - 1 else None)
        ctx.wait("vector", "mm")
        ctx.run("vector", lambda e: e.tensor_copy(lg[:], ps_lg[:]), inc="ve")
        for blk in range(TT // 128):
            tok0 = blk * 128
            ctx.wait("tensor", "ve")
            ctx.run("tensor", lambda e, tok0=tok0: e.transpose(
                ps_t1[:, 0:NE], lg[:, tok0:tok0 + 128], idt[0:NE, 0:NE]), inc="mm")
            ctx.wait("vector", "mm")
            ctx.run("vector", lambda e: e.tensor_copy(ltm[:], ps_t1[:, 0:NE]), inc="ve")
            ctx.wait("scalar", "ve")
            ctx.run("scalar", lambda e: e.copy(ltm2[:], ltm[:]), inc="ac")
            ctx.wait("vector", "ac")
            ctx.run("vector", lambda e: e.reduce_max(m1[:, 0:1], ltm2[:], axis=AX), inc="ve")
            ctx.wait("scalar", "ve")
            ctx.run("scalar", lambda e: e.copy(bounce[:, 0:1], m1[:, 0:1]))
            ctx.run("scalar", lambda e: e.mul(bounce[:, 1:2], m1[:, 0:1], -1.0), inc="ac")
            ctx.wait("vector", "ac")
            ctx.run("vector", lambda e: e.tensor_scalar(
                cstage[:], ltm2[:], bounce[:, 0:1], 0.0, ALU.subtract, ALU.is_ge), inc="ve")
            ctx.wait("scalar", "ve")
            ctx.run("scalar", lambda e: e.copy(cst2[:], cstage[:]), inc="ac")
            ctx.wait("vector", "ac")
            ctx.run("vector", lambda e: e.scalar_tensor_tensor(
                l2[:], cst2[:], -1e30, ltm2[:], ALU.mult, ALU.add), inc="ve")
            ctx.wait("scalar", "ve")
            ctx.run("scalar", lambda e: e.copy(l22[:], l2[:]), inc="ac")
            ctx.wait("vector", "ac")
            ctx.run("vector", lambda e: e.reduce_max(m1[:, 1:2], l22[:], axis=AX), inc="ve")
            ctx.wait("scalar", "ve")
            ctx.run("scalar", lambda e: e.copy(bounce[:, 2:3], m1[:, 1:2]))
            ctx.run("scalar", lambda e: e.activation(m1[:, 2:3], bounce[:, 2:3], AF.Exp, bias=bounce[:, 1:2]))
            ctx.run("scalar", lambda e: e.add(m1[:, 3:4], m1[:, 2:3], 1.0), inc="ac")
            ctx.wait("vector", "ac")
            ctx.run("vector", lambda e: e.reciprocal(w12[:, 0:1], m1[:, 3:4]), inc="ve")
            ctx.wait("scalar", "ve")
            ctx.run("scalar", lambda e: e.copy(w12[:, 2:3], w12[:, 0:1]), inc="ac")
            ctx.wait("vector", "ac")
            ctx.run("vector", lambda e: e.tensor_tensor(w12[:, 1:2], m1[:, 2:3], w12[:, 2:3], ALU.mult), inc="ve")
            ctx.wait("scalar", "ve")
            ctx.run("scalar", lambda e: e.copy(w12[:, 3:4], w12[:, 1:2]), inc="ac")
            ctx.wait("vector", "ac")
            ctx.run("vector", lambda e: e.tensor_scalar(
                c1m[:], cst2[:], w12[:, 2:3], None, ALU.mult))
            ctx.run("vector", lambda e: e.tensor_scalar(
                m2m[:], l22[:], bounce[:, 2:3], 0.0, ALU.subtract, ALU.is_ge), inc="ve")
            ctx.wait("scalar", "ve")
            ctx.run("scalar", lambda e: e.copy(c1m2[:], c1m[:]))
            ctx.run("scalar", lambda e: e.copy(m2m2[:], m2m[:]), inc="ac")
            ctx.wait("vector", "ac")
            ctx.run("vector", lambda e: e.scalar_tensor_tensor(
                cstage[:], m2m2[:], w12[:, 3:4], c1m2[:], ALU.mult, ALU.add), inc="ve")
            ctx.wait("scalar", "ve")
            ctx.run("scalar", lambda e: e.copy(cst3[:], cstage[:]), inc="ac")
            ctx.wait("tensor", "ac")
            ctx.wait("tensor", "ve")
            ctx.run("tensor", lambda e: e.transpose(ps_t2[:, :], cst3[:], idt[:]), inc="mm")
            ctx.wait("vector", "mm")
            ctx.run("vector", lambda e: e.tensor_copy(cfm[:], ps_t2[:]), inc="ve")
            ctx.wait("sync", "ve")
            ctx.store(outv[:, tt * TT + tok0: tt * TT + tok0 + 128], cfm[:])
            if tt == 0:
                ctx.store(ctx.aps["dbg3"].rearrange("p (b e) -> p b e", e=NE)[:, blk, :], ltm[:])
                ctx.store(ctx.aps["dbg4"].rearrange("p (b e) -> p b e", e=NE)[:, blk, :], cstage[:])
                ctx.store(ctx.aps["dbg5"].rearrange("p (b f) -> p b f", f=4)[:, blk, :], m1[:])


@stage("moe", ["xn1_bf", "comb", "x1", "sl1"]
       + [f"wgT{e}" for e in range(NE)] + [f"wuT{e}" for e in range(NE)]
       + [f"wdT{e}" for e in range(NE)], ["moe", "out_fm"])
def emit_moe(ctx):
    EIC = EI // 128
    xn_sb = ctx.sbuf("xn", (128, NC, TL), BF16)
    ctx.dma("sync", xn_sb[:], fm_view(ctx.aps["xn1_bf"], NC))
    cbrow = ctx.sbuf("cbrow", (1, TT), F32)
    onesr = emit_ones(ctx, F32, (1, 128))
    wgc = ctx.sbuf("wgc", (128, NC, EI), BF16)
    wuc = ctx.sbuf("wuc", (128, NC, EI), BF16)
    wdc = ctx.sbuf("wdc", (128, EIC, D), BF16)
    ps_g = [ctx.psum(f"g{i}", (128, TT), F32) for i in range(2)]
    ps_u = [ctx.psum(f"u{i}", (128, TT), F32) for i in range(2)]
    ps_d = [ctx.psum(f"d{i}", (128, TT), F32) for i in range(2)]
    ps_cb = ctx.psum("cb", (128, TT), F32)
    cb_sb = ctx.sbuf("cb", (128, TT), F32)
    tA = ctx.sbuf("tA", (128, 2, TT), F32)
    he = ctx.sbuf("he", (128, EIC, TT), BF16)
    dstg = ctx.sbuf("dstg", (128, 2, TT), F32)
    moev = fm_view(ctx.aps["moe"], NC)

    moe_st_base = 0
    for e8 in range(NE):
        ctx.wait("gpsimd", "mm")
        ctx.dma("gpsimd", wgc[:], ctx.aps[f"wgT{e8}"].rearrange("(c p) o -> p c o", p=128))
        ctx.dma("gpsimd", wuc[:], ctx.aps[f"wuT{e8}"].rearrange("(c p) o -> p c o", p=128))
        ctx.dma("gpsimd", wdc[:], ctx.aps[f"wdT{e8}"].rearrange("(c p) o -> p c o", p=128))
        ld_w = ctx.cnt["ld"]
        ctx.wait("tensor", "ld", ld_w)
        for tt in range(NTT):
            sl = slice(tt * TT, (tt + 1) * TT)
            ctx.wait("sync", "mm")
            ctx.dma("sync", cbrow[:], ctx.aps["comb"][e8:e8 + 1, sl])
            ld_cb = ctx.cnt["ld"]
            ctx.wait("tensor", "ld", ld_cb)
            ctx.wait("tensor", "ve")
            ctx.run("tensor", lambda e: e.matmul(
                ps_cb[:], onesr[:], cbrow[:], start=True, stop=True), inc="mm")
            ctx.wait("vector", "mm")
            ctx.run("vector", lambda e: e.tensor_copy(cb_sb[:], ps_cb[:]), inc="ve")
            for oc in range(EIC):
                par = oc % 2
                ctx.wait("tensor", "ve")
                ctx.wait("tensor", "ac")
                for kcc in range(NC):
                    ctx.run("tensor", lambda e, oc=oc, kcc=kcc, par=par, sl=sl: e.matmul(
                        ps_g[par][:], wgc[:, kcc, oc * 128:(oc + 1) * 128], xn_sb[:, kcc, sl],
                        start=(kcc == 0), stop=(kcc == NC - 1)))
                    ctx.run("tensor", lambda e, oc=oc, kcc=kcc, par=par, sl=sl: e.matmul(
                        ps_u[par][:], wuc[:, kcc, oc * 128:(oc + 1) * 128], xn_sb[:, kcc, sl],
                        start=(kcc == 0), stop=(kcc == NC - 1)),
                        inc="mm" if kcc == NC - 1 else None)
                ctx.wait("scalar", "mm")
                ctx.wait("scalar", "ve")
                ctx.run("scalar", lambda e, par=par: e.activation(tA[:, par, :], ps_g[par][:], AF.Silu), inc="ac")
                ctx.wait("vector", "mm")
                ctx.wait("vector", "ac")
                ctx.run("vector", lambda e, par=par: e.tensor_tensor(tA[:, par, :], tA[:, par, :], ps_u[par][:], ALU.mult))
                ctx.run("vector", lambda e, oc=oc, par=par: e.tensor_tensor(
                    he[:, oc, :], tA[:, par, :], cb_sb[:], ALU.mult), inc="ve")
            for o2 in range(NC):
                par = o2 % 2
                ctx.wait("tensor", "ve")
                for kcc in range(EIC):
                    ctx.run("tensor", lambda e, o2=o2, kcc=kcc, par=par: e.matmul(
                        ps_d[par][:], wdc[:, kcc, o2 * 128:(o2 + 1) * 128], he[:, kcc, :],
                        start=(kcc == 0), stop=(kcc == EIC - 1)),
                        inc="mm" if kcc == EIC - 1 else None)
                ctx.wait("vector", "mm")
                ctx.wait("vector", "st")
                ctx.run("vector", lambda e, par=par: e.tensor_copy(dstg[:, par, :], ps_d[par][:]), inc="ve")
                ctx.wait("gpsimd", "ve")
                ctx.wait("gpsimd", "st", moe_st_base)
                kw = {} if e8 == 0 else {"accum_op": ALU.add}
                ctx.run("gpsimd", lambda e, o2=o2, par=par, sl=sl, kw=kw: e.dma_start(
                    out=moev[:, o2, sl], in_=dstg[:, par, :], **kw), inc="st", amt=16)
        moe_st_base = ctx.cnt["st"]

    for _h in (dstg, he, tA, cb_sb, wdc, wuc, wgc):
        ctx.sbuf_free(_h)
    x1_t = ctx.sbuf("x1t", (128, NC, TT), F32)
    s_t = ctx.sbuf("st2", (128, NC, TT), F32)
    mo_t = ctx.sbuf("mot", (128, NC, TT), F32)
    og = ctx.sbuf("og", (128, NC, TT), F32)
    outv = fm_view(ctx.aps["out_fm"], NC)
    ctx.wait("sync", "st")
    for tt in range(NTT):
        sl = slice(tt * TT, (tt + 1) * TT)
        ctx.wait("sync", "ac")
        ctx.wait("sync", "ve")
        ctx.dma("sync", x1_t[:], fm_view(ctx.aps["x1"], NC)[:, :, sl])
        ctx.dma("sync", s_t[:], fm_view(ctx.aps["sl1"], NC)[:, :, sl])
        ctx.dma("sync", mo_t[:], moev[:, :, sl])
        ld_f = ctx.cnt["ld"]
        ctx.wait("scalar", "ld", ld_f)
        for c in range(NC):
            ctx.run("scalar", lambda e, c=c: e.activation(og[:, c, :], mo_t[:, c, :], AF.Silu),
                    inc="ac" if c == NC - 1 else None)
        ctx.wait("vector", "ac")
        ctx.wait("vector", "st")
        for c in range(NC):
            ctx.run("vector", lambda e, c=c: e.tensor_tensor(og[:, c, :], og[:, c, :], s_t[:, c, :], ALU.mult))
            ctx.run("vector", lambda e, c=c: e.tensor_tensor(og[:, c, :], og[:, c, :], x1_t[:, c, :], ALU.add), inc="ve")
        ctx.wait("sync", "ve")
        ctx.store(outv[:, :, sl], og[:])


def build_stage_program(name, reads, writes, emit, cc_names):
    nc = bass.Bass("TRN2", target_bir_lowering=False)
    ctx = Ctx(nc)
    for r in reads:
        shp, dt = TENSORS[r]
        ctx.aps[r] = nc.dram_tensor(r, shp, dt, kind="ExternalInput").ap()
    for w in writes:
        shp, dt = TENSORS[w]
        ctx.aps[w] = nc.dram_tensor(w, shp, dt, kind="ExternalOutput").ap()
    for c in cc_names:
        shp, dt = CC_TENSORS[c]
        ctx.aps[c] = nc.dram_tensor(c, shp, dt, kind="Internal").ap()
    emit(ctx)
    if ctx.cnt["st"] > 0:
        ctx.wait("sync", "st")
    with nc.Block() as block:
        @block.sync
        def _(e):
            for fn in ctx.cmds["sync"]:
                fn(e)

        @block.tensor
        def _(e):
            for fn in ctx.cmds["tensor"]:
                fn(e)

        @block.vector
        def _(e):
            for fn in ctx.cmds["vector"]:
                fn(e)

        @block.scalar
        def _(e):
            for fn in ctx.cmds["scalar"]:
                fn(e)

        @block.gpsimd
        def _(e):
            for fn in ctx.cmds["gpsimd"]:
                fn(e)
    return nc


def run_unfused(host_vals, upto=None, start=None):
    started = start is None
    import os
    dbg = os.environ.get("KDBG")
    for (name, reads, writes, emit, cc_names) in STAGES:
        if not started:
            if name == start:
                started = True
            else:
                continue
        if dbg:
            import sys
            print(f"[stage {name}]", flush=True, file=sys.stderr)
        nc = build_stage_program(name, reads, writes, emit, cc_names)
        in_maps = [{r: np.ascontiguousarray(host_vals[r][c]) for r in reads}
                   for c in range(NCORES)]
        res = run_bass_kernel_spmd(nc, in_maps, core_ids=list(range(NCORES)))
        for w in writes:
            host_vals[w] = [res.results[c][w] for c in range(NCORES)]
        if upto is not None and name == upto:
            break
    return host_vals


def _cols(v):
    return np.ascontiguousarray(np.asarray(v, np.float32).reshape(-1, 128).T)


def make_constants():
    import jax
    import jax.numpy as jnp
    cpu = jax.devices("cpu")[0]
    with jax.default_device(cpu):
        idx = jnp.linspace(0.0, 1.0, HDIM)
        freqs = (10.0 ** idx)[None, :] * jnp.arange(H, dtype=jnp.float32)[:, None]
        t = jnp.arange(T, dtype=jnp.float32)[:, None, None]
        angles = t * freqs[None]
        cos_tab = np.asarray(jnp.cos(angles), np.float32).reshape(T, D)
    cos_fm_full = np.ascontiguousarray(cos_tab.T)
    triu = np.triu(np.ones((128, 128), np.float32))
    import ml_dtypes
    triu_bf = triu.astype(ml_dtypes.bfloat16)
    t_glob = np.arange(T, dtype=np.float32)
    return cos_fm_full, triu_bf, t_glob


def prepare_host_vals(inputs):
    ii = {k: np.asarray(v) for k, v in inputs.items()}
    cos_fm_full, triu_bf, t_glob = make_constants()

    vals = {}

    def rep(name, arr):
        a = np.ascontiguousarray(arr)
        vals[name] = [a] * NCORES

    WT = lambda w: np.ascontiguousarray(np.asarray(w, np.float32).T)
    rep("w1T", WT(ii["L0_attn_W1"]))
    rep("w2T", WT(ii["L0_attn_W2"]))
    rep("fw1T", WT(ii["L0_ffn_w1_w"]))
    rep("fw2T", WT(ii["L0_ffn_w2_w"]))
    rep("fh1T", WT(ii["L0_ffn_h1_w"]))
    rep("fh2T", WT(ii["L0_ffn_h2_w"]))
    rep("fw3T", WT(ii["L0_ffn_w3_w"]))
    rep("fs1", _cols(ii["L0_ffn_w1_s"]))
    rep("fs2", _cols(ii["L0_ffn_w2_s"]))
    rep("fsh1", _cols(ii["L0_ffn_h1_s"]))
    rep("fsh2", _cols(ii["L0_ffn_h2_s"]))
    rep("fs3", _cols(ii["L0_ffn_w3_s"]))
    rep("ln1w", _cols(ii["L1_attn_ln1_w"]))
    rep("ln1b", _cols(ii["L1_attn_ln1_b"]))
    cw = np.asarray(ii["L1_attn_conv_w"], np.float32)[:, 0, :]
    rep("convw", np.ascontiguousarray(cw.reshape(NC, 128, 4).transpose(1, 0, 2)))
    rep("convb", _cols(ii["L1_attn_conv_b"]))
    rep("qkT", WT(ii["L1_attn_qk_w"]))
    rep("vT", WT(ii["L1_attn_v_w"]))
    rep("gT", WT(ii["L1_attn_g_w"]))
    rep("gb", _cols(ii["L1_attn_g_b"]))
    rep("oT", WT(ii["L1_attn_o_w"]))
    rep("onw", _cols(ii["L1_attn_onorm_w"]))
    rep("onb", _cols(ii["L1_attn_onorm_b"]))
    rep("gammaA", np.asarray(ii["L1_attn_gamma_attn"], np.float32).reshape(16, 1))
    rep("ln2w", _cols(ii["L1_attn_ln2_w"]))
    rep("ln2b", _cols(ii["L1_attn_ln2_b"]))
    rep("gateT", WT(ii["L1_attn_gate_w"]))
    rep("upT", WT(ii["L1_attn_up_w"]))
    rep("downT", WT(ii["L1_attn_down_w"]))
    rep("gamma1", _cols(ii["L1_attn_gamma1"]))
    rep("gamma2", _cols(ii["L1_attn_gamma2"]))
    rep("routT", WT(ii["L1_ffn_router_w"]))
    for e in range(NE):
        rep(f"wgT{e}", WT(ii["L1_ffn_Wg"][e]))
        rep(f"wuT{e}", WT(ii["L1_ffn_Wu"][e]))
        rep(f"wdT{e}", WT(ii["L1_ffn_Wd"][e]))

    x = np.asarray(ii["L0_attn_x"], np.float32)
    vals["x_fm"] = []
    vals["cos_fm"] = []
    vals["t16"] = []
    vals["rankmask"] = []
    vals["halosel"] = []
    vals["haloflag"] = []
    for c in range(NCORES):
        b, j = c // NJ, c % NJ
        sl = slice(j * TL, (j + 1) * TL)
        vals["x_fm"].append(np.ascontiguousarray(x[b, sl, :].T))
        cf = cos_fm_full[:, sl]
        vals["cos_fm"].append(np.ascontiguousarray(cf.reshape(NC, 128, TL).transpose(1, 0, 2)))
        vals["t16"].append(np.ascontiguousarray(np.broadcast_to(t_glob[sl][None, :], (16, TL))))
        rm = np.zeros((128, 4), np.float32)
        rm[:, :j] = 1.0
        vals["rankmask"].append(rm)
        hs = np.zeros((128, 4), np.float32)
        if j > 0:
            hs[:, j - 1] = 1.0
        vals["halosel"].append(hs)
        vals["haloflag"].append(np.full((128, 1), 1.0 if j > 0 else 0.0, np.float32))
    rep("triu", triu_bf)
    rep("ident", np.eye(128, dtype=np.float32))
    import ml_dtypes
    heads = (np.arange(D) // HDIM)
    ind_bc = (heads[None, :] == np.arange(H)[:, None]).astype(np.float32)
    rep("ind_bc", ind_bc)
    ind_sq = np.zeros((128, NC * 16), np.float32)
    for c in range(NC):
        for p in range(128):
            m = (c * 128 + p) // HDIM
            ind_sq[p, c * 16 + m] = 1.0
    rep("ind_sq", ind_sq.astype(ml_dtypes.bfloat16))
    return vals


def kernel(**inputs):
    vals = prepare_host_vals(inputs)
    vals = run_unfused(vals)
    out = np.empty((B, T, D), np.float32)
    for c in range(NCORES):
        b, j = c // NJ, c % NJ
        out[b, j * TL:(j + 1) * TL, :] = vals["out_fm"][c].T
    return out
